# version 01
# baseline (speedup 1.0000x reference)
import contextlib
import numpy as np
import ml_dtypes
import concourse.bass as bass
import concourse.mybir as mybir
from concourse.bass_utils import run_bass_kernel_spmd

F32 = mybir.dt.float32
BF16 = mybir.dt.bfloat16
I32 = mybir.dt.int32
AF = mybir.ActivationFunctionType
ALU = mybir.AluOpType
AX = mybir.AxisListType

D = 1024
SEQ = 8192
TPC = 2048
NCH = 8
NTG = 4
NSTR = 16
BFD = np.dtype(ml_dtypes.bfloat16)


class SemCnt:
    def __init__(self, h):
        self.h = h
        self.n = 0

    def inc(self, ins, k=1):
        self.n += k
        ins.then_inc(self.h, k)


class Ctx:
    def __init__(self, nc):
        self.nc = nc
        self.st = contextlib.ExitStack()
        self._id = 0

    def sb(self, name, shape, dt):
        return self.st.enter_context(self.nc.sbuf_tensor(name, shape, dt))

    def sem(self, name):
        if not hasattr(self, "_sems"):
            self._sems = {}
        if name not in self._sems:
            self._sems[name] = SemCnt(self.st.enter_context(self.nc.semaphore(f"sem_{name}")))
        return self._sems[name]

    def uid(self):
        self._id += 1
        return self._id

    def close(self):
        self.st.close()




def emit_consts(ctx, C):
    nc = ctx.nc
    with nc.Block() as block:
        s = ctx.sem("cst")

        @block.gpsimd
        def _(g):
            g.memset(C["ones_col"][:], 1.0)
            g.memset(C["ones_row"][:], 1.0)
            g.iota(C["eye_bf"][:, 0:128], [[1, 128]], base=0, channel_multiplier=-1,
                   allow_small_or_imprecise_dtypes=True)
            if "eye_f32" in C:
                g.iota(C["eye_f32"][:, 0:128], [[1, 128]], base=0, channel_multiplier=-1,
                       allow_small_or_imprecise_dtypes=True)
            s.inc(g.engine_nop(), 1)

        @block.vector
        def _(v):
            v.wait_ge(s.h, 1)
            v.tensor_scalar(C["eye_bf"][:], C["eye_bf"][:], 0.0, None, ALU.is_equal)
            if "eye_f32" in C:
                v.tensor_scalar(C["eye_f32"][:], C["eye_f32"][:], 0.0, None, ALU.is_equal)


def emit_load_chunks(ctx, dram, sb_chunks, rows=D):
    nc = ctx.nc
    with nc.Block() as block:
        sd = ctx.sem("ld")

        @block.sync
        def _(sy):
            for c in range(rows // 128):
                sd.inc(sy.dma_start(out=sb_chunks[c][:], in_=dram[128 * c:128 * (c + 1), :]), 16)
            sy.wait_ge(sd.h, sd.n)


def emit_store_chunks(ctx, sb_chunks, dram, rows=D):
    nc = ctx.nc
    with nc.Block() as block:
        sd = ctx.sem("st")

        @block.sync
        def _(sy):
            for c in range(rows // 128):
                sd.inc(sy.dma_start(out=dram[128 * c:128 * (c + 1), :], in_=sb_chunks[c][:]), 16)
            sy.wait_ge(sd.h, sd.n)


def emit_ln(ctx, x_sb, out_sb, C, SCR, extra_scale=1.0):
    nc = ctx.nc
    xsq, rows, tmp = SCR["xsq"], SCR["rows"], SCR["tmp"]
    ones_col, ones_row = C["ones_col"], C["ones_row"]
    uid = ctx.uid()

    def RM(lo, hi):
        return rows[0:1, lo:hi]

    def RR(lo, hi):
        return rows[0:1, TPC + lo:TPC + hi]

    with contextlib.ExitStack() as pst:
        stat = [(pst.enter_context(nc.psum_tensor(f"ls1_{uid}_{tg}", [1, 512], F32)),
                 pst.enter_context(nc.psum_tensor(f"ls2_{uid}_{tg}", [1, 512], F32)))
                for tg in range(NTG)]
        with nc.Block() as block:
            sv, ss, sm = ctx.sem("lv"), ctx.sem("ls"), ctx.sem("lm")
            bv, bs, bm = sv.n, ss.n, sm.n

            @block.vector
            def _(v):
                for c in range(NCH):
                    ins = v.tensor_copy(out_sb[c][:], x_sb[c][:])
                sv.inc(ins, 1)

            @block.scalar
            def _(s):
                s.wait_ge(sv.h, bv + 1)
                for c in range(NCH):
                    if c >= 1:
                        s.wait_ge(sm.h, bm + c)
                    ss.inc(s.activation(xsq[:], out_sb[c][:], AF.Square), 1)

            @block.tensor
            def _(t):
                for c in range(NCH):
                    t.wait_ge(ss.h, bs + c + 1)
                    for tg in range(NTG):
                        sl = slice(512 * tg, 512 * (tg + 1))
                        t.matmul(stat[tg][0][0:1, :], ones_col[:, 0:1], out_sb[c][:, sl],
                                 start=(c == 0), stop=(c == NCH - 1))
                        ins = t.matmul(stat[tg][1][0:1, :], ones_col[:, 0:1], xsq[:, sl],
                                       start=(c == 0), stop=(c == NCH - 1))
                    sm.inc(ins, 1)

            @block.vector
            def _(v):
                v.wait_ge(sm.h, bm + NCH)
                for tg in range(NTG):
                    lo, hi = 512 * tg, 512 * (tg + 1)
                    v.tensor_scalar(RM(lo, hi), stat[tg][0][0:1, :], 1.0 / D, None, ALU.mult)
                    v.tensor_scalar(RR(lo, hi), stat[tg][1][0:1, :], 1.0 / D, None, ALU.mult)
                    v.tensor_tensor(tmp[0:1, 0:512], RM(lo, hi), RM(lo, hi), ALU.mult)
                    v.tensor_tensor(RR(lo, hi), RR(lo, hi), tmp[0:1, 0:512], ALU.subtract)
                    ins = v.tensor_scalar(RR(lo, hi), RR(lo, hi), 1e-5, None, ALU.add)
                sv.inc(ins, 1)

            @block.scalar
            def _(s):
                s.wait_ge(sv.h, bv + 2)
                s.activation(RR(0, TPC), RR(0, TPC), AF.Ln)
                ins = s.activation(RR(0, TPC), RR(0, TPC), AF.Exp, scale=-0.5)
                if extra_scale != 1.0:
                    ins = s.mul(RR(0, TPC), RR(0, TPC), float(extra_scale))
                ss.inc(ins, 1)

            @block.vector
            def _(v):
                v.wait_ge(ss.h, bs + NCH + 1)
                v.tensor_tensor(RM(0, TPC), RM(0, TPC), RR(0, TPC), ALU.mult)

    with contextlib.ExitStack() as pst:
        bc = pst.enter_context(nc.psum_tensor(f"lbc_{uid}", [128, 1024], F32))
        with nc.Block() as block:
            sv, sm = ctx.sem("nv"), ctx.sem("nm")
            bv, bm = sv.n, sm.n
            for tg in range(NTG):
                sl = slice(512 * tg, 512 * (tg + 1))

                @block.tensor
                def _(t, tg=tg, sl=sl):
                    if tg >= 1:
                        t.wait_ge(sv.h, bv + tg)
                    t.matmul(bc[:, 0:512], ones_row[0:1, :], RR(sl.start, sl.stop),
                             start=True, stop=True)
                    ins = t.matmul(bc[:, 512:1024], ones_row[0:1, :], RM(sl.start, sl.stop),
                                   start=True, stop=True)
                    sm.inc(ins, 1)

                @block.vector
                def _(v, tg=tg, sl=sl):
                    v.wait_ge(sm.h, bm + tg + 1)
                    for c in range(NCH):
                        v.tensor_tensor(tmp[:], x_sb[c][:, sl], bc[:, 0:512], ALU.mult)
                        ins = v.tensor_tensor(out_sb[c][:, sl], tmp[:], bc[:, 512:1024],
                                              ALU.subtract)
                    sv.inc(ins, 1)


def wcol_ap(w_dram, row_stride, oc, base=0):
    return bass.AP(w_dram.tensor, base + 128 * oc,
                   [[row_stride, 128], [128 * row_stride, 8], [1, 128]])


def wrow_ap(w_dram, r0, base=0):
    return bass.AP(w_dram.tensor, base + r0 * D, [[D, 128], [1, D]])


def emit_matapply(ctx, w_dram, bias_dram, xln_sb, n_oc, evac, wb_sb, bias_sb, n_bias=None):
    nc = ctx.nc
    uid = ctx.uid()
    nb = n_oc if n_bias is None else n_bias
    with contextlib.ExitStack() as pst:
        cps = [pst.enter_context(nc.psum_tensor(f"mmp{uid}_{i}", [128, 512], F32))
               for i in range(2)]
        with nc.Block() as block:
            sd, sm, se = ctx.sem("md"), ctx.sem("mm"), ctx.sem("me")
            bd_, bm_, be_ = sd.n, sm.n, se.n
            bias_units = 0
            if bias_dram is not None:
                bias_units = 16

            @block.gpsimd
            def _(g):
                if bias_dram is not None:
                    sd.inc(g.dma_start(out=bias_sb[:, 0:nb],
                                       in_=bias_dram.rearrange("(c p) -> p c", p=128),
                                       allow_slow_non_contiguous=True), 16)
                for oc in range(n_oc):
                    if oc >= 2:
                        g.wait_ge(sm.h, bm_ + (oc - 1) * NTG)
                    sd.inc(g.dma_start(out=wb_sb[oc % 2][:],
                                       in_=wcol_ap(w_dram, 128 * n_oc, oc)), 16)
                g.wait_ge(sd.h, sd.n)

            @block.tensor
            def _(t):
                for oc in range(n_oc):
                    t.wait_ge(sd.h, bd_ + bias_units + 16 * (oc + 1))
                    for tg in range(NTG):
                        it = oc * NTG + tg
                        if it >= 2:
                            t.wait_ge(se.h, be_ + it - 1)
                        for j in range(NCH):
                            ins = t.matmul(cps[it % 2][:, 0:512],
                                           wb_sb[oc % 2][:, 128 * j:128 * (j + 1)],
                                           xln_sb[j][:, 512 * tg:512 * (tg + 1)],
                                           start=(j == 0), stop=(j == NCH - 1))
                        sm.inc(ins, 1)

            for oc in range(n_oc):
                for tg in range(NTG):
                    it = oc * NTG + tg
                    evac(block, oc, tg, cps[it % 2][:, 0:512], sm, bm_ + it + 1, se)


def emit_transpose_pass(ctx, src_chunks, dst_tiles, eye, n_ti):
    nc = ctx.nc
    uid = ctx.uid()
    with contextlib.ExitStack() as pst:
        tp = [pst.enter_context(nc.psum_tensor(f"tp{uid}_{i}", [128, 128], BF16))
              for i in range(2)]
        with nc.Block() as block:
            sm, sv = ctx.sem("tm"), ctx.sem("tv")
            bm_, bv_ = sm.n, sv.n

            @block.tensor
            def _(t):
                k = 0
                for ti in range(n_ti):
                    for c in range(NCH):
                        if k >= 2:
                            t.wait_ge(sv.h, bv_ + k - 1)
                        sm.inc(t.transpose(tp[k % 2][:],
                                           src_chunks[c][:, 128 * ti:128 * (ti + 1)], eye[:]), 1)
                        k += 1

            @block.vector
            def _(v):
                k = 0
                for ti in range(n_ti):
                    for c in range(NCH):
                        v.wait_ge(sm.h, bm_ + k + 1)
                        sv.inc(v.tensor_copy(dst_tiles[ti][:, 128 * c:128 * (c + 1)],
                                             tp[k % 2][:]), 1)
                        k += 1


def emit_swiglu(ctx, gate_ap, up_ap, down_ap, x_sb, rhs_gate, rhs_up, nhc, is_int,
                out_scale, name, scale_tile=None, tmp_sb=None):
    nc = ctx.nc
    uid = ctx.uid()
    nband = nhc // 8
    with contextlib.ExitStack() as fst:
        if is_int:
            wgi = fst.enter_context(nc.sbuf_tensor(f"wgi{uid}", [128, 1024], I32))
            wui = fst.enter_context(nc.sbuf_tensor(f"wui{uid}", [128, 1024], I32))
            wdi = fst.enter_context(nc.sbuf_tensor(f"wdi{uid}", [128, 1024], I32))
        wgb = [fst.enter_context(nc.sbuf_tensor(f"wgb{uid}_{i}", [128, 1024], BF16))
               for i in range(2)]
        wub = [fst.enter_context(nc.sbuf_tensor(f"wub{uid}_{i}", [128, 1024], BF16))
               for i in range(2)]
        wdb = [fst.enter_context(nc.sbuf_tensor(f"wdb{uid}_{i}", [128, 1024], BF16))
               for i in range(8)]
        g_sb = fst.enter_context(nc.sbuf_tensor(f"gsb{uid}", [128, 2048], BF16))
        hb = [fst.enter_context(nc.sbuf_tensor(f"hb{uid}_{i}", [128, 2048], BF16))
              for i in range(8)]
        gps = fst.enter_context(nc.psum_tensor(f"gps{uid}", [128, 2048], F32))
        ups = fst.enter_context(nc.psum_tensor(f"ups{uid}", [128, 2048], F32))

        with nc.Block() as block:
            sd = ctx.sem(name + "d")
            scst = ctx.sem(name + "c")
            smG = ctx.sem(name + "g")
            ssil = ctx.sem(name + "s")
            smU = ctx.sem(name + "u")
            shh = ctx.sem(name + "h")
            sdd = ctx.sem(name + "w")
            sdc = ctx.sem(name + "x")
            smD = ctx.sem(name + "D")
            sde = ctx.sem(name + "e")
            Bd, Bc, Bg, Bs, Bu, Bh, Bw, Bx, BD, Be = (sd.n, scst.n, smG.n, ssil.n, smU.n,
                                                      shh.n, sdd.n, sdc.n, smD.n, sde.n)

            @block.gpsimd
            def _(g):
                for band in range(nband):
                    for hcb in range(8):
                        hc = 8 * band + hcb
                        if is_int:
                            if hc >= 1:
                                g.wait_ge(scst.h, Bc + 2 * hc)
                            sd.inc(g.dma_start(out=wgi[:], in_=gate_ap(hc)), 16)
                            sd.inc(g.dma_start(out=wui[:], in_=up_ap(hc)), 16)
                        else:
                            if hc >= 2:
                                g.wait_ge(smG.h, Bg + hc - 1)
                                g.wait_ge(smU.h, Bu + hc - 1)
                            sd.inc(g.dma_start(out=wgb[hc % 2][:], in_=gate_ap(hc)), 16)
                            sd.inc(g.dma_start(out=wub[hc % 2][:], in_=up_ap(hc)), 16)
                    for hcb in range(8):
                        idx = band * 8 + hcb
                        if is_int:
                            if idx >= 1:
                                g.wait_ge(sdc.h, Bx + idx)
                            sdd.inc(g.dma_start(out=wdi[:], in_=down_ap(band, hcb)), 16)
                        else:
                            if band >= 1:
                                g.wait_ge(smD.h, BD + band * 32)
                            sdd.inc(g.dma_start(out=wdb[hcb][:], in_=down_ap(band, hcb)), 16)
                g.wait_ge(sd.h, sd.n)
                g.wait_ge(sdd.h, sdd.n)

            @block.tensor
            def _(t):
                for band in range(nband):
                    for hcb in range(8):
                        hc = 8 * band + hcb
                        if is_int:
                            t.wait_ge(scst.h, Bc + 2 * hc + 1)
                        else:
                            t.wait_ge(sd.h, Bd + 32 * hc + 16)
                        if hc >= 1:
                            t.wait_ge(ssil.h, Bs + hc)
                        if band >= 1 and hcb == 0:
                            t.wait_ge(sde.h, Be + band * 32)
                        for tg in range(NTG):
                            for j in range(NCH):
                                ins = t.matmul(gps[:, 512 * tg:512 * (tg + 1)],
                                               wgb[hc % 2][:, 128 * j:128 * (j + 1)],
                                               rhs_gate(j, tg),
                                               start=(j == 0), stop=(j == NCH - 1))
                        smG.inc(ins, 1)
                        if is_int:
                            t.wait_ge(scst.h, Bc + 2 * hc + 2)
                        else:
                            t.wait_ge(sd.h, Bd + 32 * hc + 32)
                        if hc >= 1:
                            t.wait_ge(shh.h, Bh + 4 * hc)
                        for tg in range(NTG):
                            for j in range(NCH):
                                ins = t.matmul(ups[:, 512 * tg:512 * (tg + 1)],
                                               wub[hc % 2][:, 128 * j:128 * (j + 1)],
                                               rhs_up(j, tg),
                                               start=(j == 0), stop=(j == NCH - 1))
                        smU.inc(ins, 1)
                    t.wait_ge(ssil.h, Bs + 8 * (band + 1))
                    t.wait_ge(shh.h, Bh + 32 * (band + 1))
                    if is_int:
                        t.wait_ge(sdc.h, Bx + 8 * (band + 1))
                    else:
                        t.wait_ge(sdd.h, Bw + 128 * (band + 1))
                    for dco in range(NCH):
                        for tg in range(NTG):
                            k = band * 32 + dco * NTG + tg
                            bank = (dco * NTG + tg) % 4
                            if k >= 4:
                                t.wait_ge(sde.h, Be + k - 3)
                            for hcb in range(8):
                                ins = t.matmul(gps[:, 512 * bank:512 * (bank + 1)],
                                               wdb[hcb][:, 128 * dco:128 * (dco + 1)],
                                               hb[hcb][:, 512 * tg:512 * (tg + 1)],
                                               start=(hcb == 0), stop=(hcb == 7))
                            smD.inc(ins, 1)

            @block.scalar
            def _(s):
                for band in range(nband):
                    for hcb in range(8):
                        hc = 8 * band + hcb
                        s.wait_ge(smG.h, Bg + hc + 1)
                        if hc >= 1:
                            s.wait_ge(shh.h, Bh + 4 * hc)
                        ssil.inc(s.activation(g_sb[:], gps[:, 0:2048], AF.Silu), 1)

            @block.vector
            def _(v):
                def hmult(j):
                    v.wait_ge(ssil.h, Bs + j + 1)
                    v.wait_ge(smU.h, Bu + j + 1)
                    if j % 8 == 0 and j >= 8:
                        v.wait_ge(smD.h, BD + 4 * j)
                    for tg in range(NTG):
                        sl = slice(512 * tg, 512 * (tg + 1))
                        shh.inc(v.tensor_tensor(hb[j % 8][:, sl], g_sb[:, sl], ups[:, sl],
                                                ALU.mult), 1)

                for band in range(nband):
                    for hcb in range(8):
                        hc = 8 * band + hcb
                        if is_int:
                            v.wait_ge(sd.h, Bd + 32 * hc + 16)
                            if hc >= 2:
                                v.wait_ge(smG.h, Bg + hc - 1)
                            v.tensor_copy(wgb[hc % 2][:], wgi[:])
                            scst.inc(v.engine_nop(), 1)
                            v.wait_ge(sd.h, Bd + 32 * hc + 32)
                            if hc >= 2:
                                v.wait_ge(smU.h, Bu + hc - 1)
                            v.tensor_copy(wub[hc % 2][:], wui[:])
                            scst.inc(v.engine_nop(), 1)
                        if hcb >= 1:
                            hmult(hc - 1)
                    hmult(8 * band + 7)
                    if is_int:
                        for hcb in range(8):
                            idx = band * 8 + hcb
                            v.wait_ge(sdd.h, Bw + 16 * (idx + 1))
                            if band >= 1:
                                v.wait_ge(smD.h, BD + band * 32)
                            sdc.inc(v.tensor_copy(wdb[hcb][:], wdi[:]), 1)
                    for dco in range(NCH):
                        for tg in range(NTG):
                            k = band * 32 + dco * NTG + tg
                            bank = (dco * NTG + tg) % 4
                            v.wait_ge(smD.h, BD + k + 1)
                            sl = slice(512 * tg, 512 * (tg + 1))
                            if scale_tile is not None:
                                v.tensor_tensor(tmp_sb[:], gps[:, 512 * bank:512 * (bank + 1)],
                                                scale_tile[:, sl], ALU.mult)
                                ins = v.tensor_tensor(x_sb[dco][:, sl], tmp_sb[:],
                                                      x_sb[dco][:, sl], ALU.add)
                            elif out_scale != 1.0:
                                ins = v.scalar_tensor_tensor(
                                    x_sb[dco][:, sl], gps[:, 512 * bank:512 * (bank + 1)],
                                    float(out_scale), x_sb[dco][:, sl], ALU.mult, ALU.add)
                            else:
                                ins = v.tensor_tensor(
                                    x_sb[dco][:, sl], gps[:, 512 * bank:512 * (bank + 1)],
                                    x_sb[dco][:, sl], ALU.add)
                            sde.inc(ins, 1)




def build_launch_a():
    nc = bass.Bass("TRN2", target_bir_lowering=False, detect_race_conditions=False)
    ctx = Ctx(nc)
    xin = nc.dram_tensor("xin", [D, TPC], F32, kind="ExternalInput").ap()
    md = {w: nc.dram_tensor(w, [D, D], F32, kind="ExternalInput").ap() for w in ("mq", "mk", "mv")}
    bd = {w: nc.dram_tensor(w, [D], F32, kind="ExternalInput").ap() for w in ("bq", "bk", "bv")}
    q0t = nc.dram_tensor("q0t", [D, TPC], BF16, kind="ExternalOutput").ap()
    k0t = nc.dram_tensor("k0t", [D, TPC], BF16, kind="ExternalOutput").ap()
    v0 = nc.dram_tensor("v0", [TPC, D], BF16, kind="ExternalOutput").ap()

    x_sb = [ctx.sb(f"xc{c}", [128, TPC], F32) for c in range(NCH)]
    xln = [ctx.sb(f"xl{c}", [128, TPC], BF16) for c in range(NCH)]
    dst = [ctx.sb(f"o{c}", [128, TPC], BF16) for c in range(NCH)]
    vtiles = [ctx.sb(f"vt{i}", [128, D], BF16) for i in range(NSTR)]
    C = {"ones_col": ctx.sb("onc", [128, 1], BF16),
         "ones_row": ctx.sb("onr", [1, 128], F32),
         "eye_bf": ctx.sb("eye", [128, 128], BF16)}
    SCR = {"xsq": ctx.sb("xsq", [128, TPC], BF16),
           "rows": ctx.sb("rows", [1, 2 * TPC], F32),
           "tmp": ctx.sb("tmpn", [128, 512], F32)}
    wb_sb = [ctx.sb(f"wb{i}", [128, D], BF16) for i in range(2)]
    bias_sb = ctx.sb("bia", [128, NCH], F32)

    emit_consts(ctx, C)
    emit_load_chunks(ctx, xin, x_sb)
    emit_ln(ctx, x_sb, xln, C, SCR)

    def bias_evac(block, oc, tg, ps_ap, sm_sem, tgt, se_sem):
        @block.scalar
        def _(s, oc=oc, tg=tg, ps_ap=ps_ap, tgt=tgt):
            s.wait_ge(sm_sem.h, tgt)
            se_sem.inc(s.activation(dst[oc][:, 512 * tg:512 * (tg + 1)], ps_ap,
                                    AF.Identity, bias=bias_sb[:, oc:oc + 1]), 1)

    for w, b, out_dram in (("mq", "bq", q0t), ("mk", "bk", k0t), ("mv", "bv", None)):
        emit_matapply(ctx, md[w], bd[b], xln, NCH, bias_evac, wb_sb, bias_sb)
        if out_dram is not None:
            emit_store_chunks(ctx, dst, out_dram)

    emit_transpose_pass(ctx, dst, vtiles, C["eye_bf"], NSTR)
    emit_store_chunks(ctx, vtiles, v0, rows=TPC)
    ctx.close()
    return nc




def emit_l0_attention(ctx, q0t_d, k0f, v0a, o_out, C):
    nc = ctx.nc
    with contextlib.ExitStack() as ast:
        q0t = [ast.enter_context(nc.sbuf_tensor(f"q0_{c}", [128, TPC], BF16))
               for c in range(NCH)]
        kt_b = [ast.enter_context(nc.sbuf_tensor(f"ktb{i}", [128, 4096], BF16)) for i in range(2)]
        va_b = [ast.enter_context(nc.sbuf_tensor(f"vab{i}", [128, 4160], BF16)) for i in range(2)]
        pt_b = [ast.enter_context(nc.sbuf_tensor(f"ptb{i}", [128, 1024], BF16)) for i in range(2)]
        den_sb = ast.enter_context(nc.sbuf_tensor("den", [1, 512], F32))
        bcb_sb = ast.enter_context(nc.sbuf_tensor("bcb", [64, 512], F32))
        s_ps = [ast.enter_context(nc.psum_tensor(f"sps{i}", [128, 1024], F32)) for i in range(2)]
        o_ps = [ast.enter_context(nc.psum_tensor(f"opsl{i}", [65, 512], F32)) for i in range(4)]

        emit_load_chunks(ctx, q0t_d, q0t)

        with nc.Block() as block:
            sdma = ctx.sem("adma")
            smS = ctx.sem("asm")
            sexp = ctx.sem("aex")
            smPV = ctx.sem("apv")
            snrm = ctx.sem("anr")
            sden = ctx.sem("adn")
            sbc = ctx.sem("abc")

            for qg in range(NTG):
                qsl = slice(512 * qg, 512 * (qg + 1))
                for pg in range(4):
                    chunks = (2 * pg, 2 * pg + 1)
                    p = qg * 4 + pg
                    sb = p * 128
                    pvb = p * 256

                    @block.sync
                    def _(sy, p=p, pvb=pvb):
                        for km in range(16):
                            if km >= 2:
                                sy.wait_ge(smPV.h, pvb + 16 * (km - 1))
                            elif p > 0:
                                sy.wait_ge(smPV.h, pvb)
                            sdma.inc(sy.dma_start(
                                out=kt_b[km % 2][:],
                                in_=bass.AP(k0f.tensor, 512 * km,
                                            [[SEQ, 128], [128 * SEQ, 8], [1, 512]])), 16)
                            sdma.inc(sy.dma_start(
                                out=va_b[km % 2][:],
                                in_=bass.AP(v0a.tensor, 1040 * 512 * km,
                                            [[1040, 128], [1040 * 128, 4], [1, 1040]])), 16)

                    def pv_of(t, i, p=p, sb=sb, chunks=chunks):
                        km, kc, ci = i // 8, (i // 2) % 4, i % 2
                        t.wait_ge(sexp.h, sb + i + 1)
                        vb = va_b[km % 2]
                        c = chunks[ci]
                        first = (km == 0 and kc == 0)
                        last = (km == 15 and kc == 3)
                        for half in range(2):
                            h = 2 * c + half
                            hidx = 2 * ci + half
                            voff = 1040 * kc + 65 * h
                            ins = t.matmul(o_ps[hidx][0:65, 0:512], vb[:, voff:voff + 65],
                                           pt_b[i % 2][:, 512 * half:512 * (half + 1)],
                                           start=first, stop=last)
                            smPV.inc(ins, 1)

                    @block.tensor
                    def _(t, p=p, sb=sb, qsl=qsl, chunks=chunks, pv_of=pv_of):
                        if p > 0:
                            t.wait_ge(snrm.h, p * 4)
                        for i in range(128):
                            km, kc, ci = i // 8, (i // 2) % 4, i % 2
                            if kc == 0 and ci == 0:
                                t.wait_ge(sdma.h, p * 512 + 32 * (km + 1))
                            if i >= 2:
                                t.wait_ge(sexp.h, sb + i - 1)
                            c = chunks[ci]
                            kb = kt_b[km % 2]
                            koff = 512 * c + 128 * kc
                            t.matmul(s_ps[i % 2][:, 0:512], kb[0:64, koff:koff + 128],
                                     q0t[c][0:64, qsl], start=True, stop=True)
                            ins = t.matmul(s_ps[i % 2][:, 512:1024], kb[64:128, koff:koff + 128],
                                           q0t[c][64:128, qsl], start=True, stop=True)
                            smS.inc(ins, 1)
                            if i >= 1:
                                pv_of(t, i - 1)
                        pv_of(t, 127)

                    @block.scalar
                    def _(s, p=p, sb=sb, pvb=pvb):
                        for i in range(128):
                            s.wait_ge(smS.h, sb + i + 1)
                            if i >= 2:
                                s.wait_ge(smPV.h, pvb + 2 * (i - 1))
                            sexp.inc(s.activation(pt_b[i % 2][:], s_ps[i % 2][:, 0:1024],
                                                  AF.Exp, scale=0.125), 1)

                    for ci, c in enumerate(chunks):
                        for half in range(2):
                            hidx = 2 * ci + half

                            @block.vector
                            def _(v, p=p, pvb=pvb, hidx=hidx):
                                v.wait_ge(smPV.h, pvb + 256)
                                if p * 4 + hidx > 0:
                                    v.wait_ge(sbc.h, p * 4 + hidx)
                                sden.inc(v.tensor_copy(den_sb[0:1, :],
                                                       o_ps[hidx][64:65, 0:512]), 1)

                            @block.scalar
                            def _(s, p=p, hidx=hidx):
                                s.wait_ge(sden.h, p * 8 + 2 * hidx + 1)
                                s.activation(den_sb[0:1, :], den_sb[0:1, :], AF.Ln)
                                sden.inc(s.activation(den_sb[0:1, :], den_sb[0:1, :],
                                                      AF.Exp, scale=-1.0), 1)

                            @block.tensor
                            def _(t, p=p, hidx=hidx):
                                t.wait_ge(sden.h, p * 8 + 2 * hidx + 2)
                                if p * 4 + hidx > 0:
                                    t.wait_ge(snrm.h, p * 4 + hidx)
                                sbc.inc(t.matmul(s_ps[0][0:64, 0:512], C["ones_row"][0:1, 0:64],
                                                 den_sb[0:1, :], start=True, stop=True), 1)

                            @block.vector
                            def _(v, p=p, hidx=hidx, c=c, half=half, qg=qg):
                                v.wait_ge(sbc.h, p * 4 + hidx + 1)
                                v.tensor_copy(bcb_sb[:], s_ps[0][0:64, 0:512])
                                ins = v.tensor_tensor(
                                    o_out[c][64 * half:64 * half + 64, 512 * qg:512 * (qg + 1)],
                                    o_ps[hidx][0:64, 0:512], bcb_sb[:], ALU.mult)
                                snrm.inc(ins, 1)

            @block.sync
            def _(sy):
                sy.wait_ge(sdma.h, sdma.n)


def emit_qkv1(ctx, w1, b1, xln, C, bias_sb, q1t_d, k1t_d, v1_d):
    nc = ctx.nc
    uid = ctx.uid()
    with contextlib.ExitStack() as qst:
        wb_sb = [qst.enter_context(nc.sbuf_tensor(f"qw{uid}_{i}", [128, 1024], BF16))
                 for i in range(2)]
        qk_sb = [qst.enter_context(nc.sbuf_tensor(f"qk{uid}_{i}", [128, 2048], BF16))
                 for i in range(2)]
        vfull = [qst.enter_context(nc.sbuf_tensor(f"vf{uid}_{i}", [128, 2048], BF16))
                 for i in range(8)]
        vtiles = [qst.enter_context(nc.sbuf_tensor(f"vq{uid}_{i}", [128, D], BF16))
                  for i in range(NSTR)]
        sout = ctx.sem("qo")
        se_base = ctx.sem("me").n
        sout_base = sout.n

        def evac(block, oc, tg, ps_ap, sm_sem, tgt, se_sem):
            @block.scalar
            def _(s, oc=oc, tg=tg, ps_ap=ps_ap, tgt=tgt):
                s.wait_ge(sm_sem.h, tgt)
                if oc < 16 and tg == 0 and oc >= 2:
                    s.wait_ge(sout.h, sout_base + 16 * (oc - 1))
                sl = slice(512 * tg, 512 * (tg + 1))
                dest = qk_sb[oc % 2] if oc < 16 else vfull[oc - 16]
                se_sem.inc(s.activation(dest[:, sl], ps_ap, AF.Identity,
                                        bias=bias_sb[:, oc:oc + 1]), 1)

            if oc < 16 and tg == NTG - 1:
                @block.sync
                def _(sy, oc=oc):
                    sy.wait_ge(se_sem.h, se_base + (oc + 1) * NTG)
                    dram = q1t_d if oc < 8 else k1t_d
                    r = (oc % 8) * 128
                    sout.inc(sy.dma_start(out=dram[r:r + 128, :], in_=qk_sb[oc % 2][:]), 16)

        emit_matapply(ctx, w1, b1, xln, 24, evac, wb_sb, bias_sb, n_bias=24)
        emit_transpose_pass(ctx, vfull, vtiles, C["eye_bf"], NSTR)
        emit_store_chunks(ctx, vtiles, v1_d, rows=TPC)
        with nc.Block() as block:
            @block.sync
            def _(sy):
                sy.wait_ge(sout.h, sout_base + 16 * 16)


B_STOP = None


def build_launch_b():
    global B_STOP
    nc = bass.Bass("TRN2", target_bir_lowering=False, detect_race_conditions=False)
    ctx = Ctx(nc)
    xin = nc.dram_tensor("xin", [D, TPC], F32, kind="ExternalInput").ap()
    q0t_d = nc.dram_tensor("q0t", [D, TPC], BF16, kind="ExternalInput").ap()
    k0f = nc.dram_tensor("k0f", [D, SEQ], BF16, kind="ExternalInput").ap()
    v0a = nc.dram_tensor("v0a", [SEQ, 1040], BF16, kind="ExternalInput").ap()
    mo = nc.dram_tensor("mo", [D, D], F32, kind="ExternalInput").ap()
    bo = nc.dram_tensor("bo", [D], F32, kind="ExternalInput").ap()
    wgT = nc.dram_tensor("wgT", [D, 3072], I32, kind="ExternalInput").ap()
    wuT = nc.dram_tensor("wuT", [D, 3072], I32, kind="ExternalInput").ap()
    wdT = nc.dram_tensor("wdT", [3072, D], I32, kind="ExternalInput").ap()
    w1 = nc.dram_tensor("w1", [D, 3072], F32, kind="ExternalInput").ap()
    b1 = nc.dram_tensor("b1", [3072], F32, kind="ExternalInput").ap()
    x3_d = nc.dram_tensor("x3", [D, TPC], F32, kind="ExternalOutput").ap()
    q1t_d = nc.dram_tensor("q1t", [D, TPC], BF16, kind="ExternalOutput").ap()
    k1t_d = nc.dram_tensor("k1t", [D, TPC], BF16, kind="ExternalOutput").ap()
    v1_d = nc.dram_tensor("v1", [TPC, D], BF16, kind="ExternalOutput").ap()

    x_sb = [ctx.sb(f"xc{c}", [128, TPC], F32) for c in range(NCH)]
    xln = [ctx.sb(f"xl{c}", [128, TPC], BF16) for c in range(NCH)]
    C = {"ones_col": ctx.sb("onc", [128, 1], BF16),
         "ones_row": ctx.sb("onr", [1, 128], F32),
         "eye_bf": ctx.sb("eye", [128, 128], BF16)}
    SCR = {"xsq": ctx.sb("xsq", [128, TPC], BF16),
           "rows": ctx.sb("rows", [1, 2 * TPC], F32),
           "tmp": ctx.sb("tmpn", [128, 512], F32)}
    bias_sb = ctx.sb("bia", [128, 24], F32)
    wbo_sb = [ctx.sb(f"wbo{i}", [128, 1024], BF16) for i in range(2)]

    emit_consts(ctx, C)
    emit_load_chunks(ctx, xin, x_sb)
    emit_l0_attention(ctx, q0t_d, k0f, v0a, xln, C)

    def dbg_dump_xln():
        with nc.Block() as block:
            sdbg = ctx.sem("dbg")
            @block.vector
            def _(v):
                for c in range(NCH):
                    sdbg.inc(v.tensor_copy(SCR["xsq"][:], xln[c][:]), 1)
            @block.sync
            def _(sy):
                for c in range(NCH):
                    sdbg.inc(sy.dma_start(out=q1t_d[128 * c:128 * (c + 1), :], in_=xln[c][:]), 16)
                sy.wait_ge(sdbg.h, sdbg.n)

    def dbg_dump_x():
        emit_store_chunks(ctx, x_sb, x3_d)

    if B_STOP == "attn":
        dbg_dump_xln(); dbg_dump_x(); ctx.close(); return nc

    def resid_evac(block, oc, tg, ps_ap, sm_sem, tgt, se_sem):
        @block.vector
        def _(v, oc=oc, tg=tg, ps_ap=ps_ap, tgt=tgt):
            v.wait_ge(sm_sem.h, tgt)
            sl = slice(512 * tg, 512 * (tg + 1))
            se_sem.inc(v.scalar_tensor_tensor(x_sb[oc][:, sl], ps_ap, bias_sb[:, oc:oc + 1],
                                              x_sb[oc][:, sl], ALU.add, ALU.add), 1)

    emit_matapply(ctx, mo, bo, xln, NCH, resid_evac, wbo_sb, bias_sb)
    if B_STOP == "resid":
        dbg_dump_x(); ctx.close(); return nc
    emit_ln(ctx, x_sb, xln, C, SCR, extra_scale=1.0 / 127.0)
    if B_STOP == "ln2":
        dbg_dump_xln(); dbg_dump_x(); ctx.close(); return nc
    emit_swiglu(ctx,
                gate_ap=lambda hc: wcol_ap(wgT, 3072, hc),
                up_ap=lambda hc: wcol_ap(wuT, 3072, hc),
                down_ap=lambda band, hcb: wrow_ap(wdT, 128 * (8 * band + hcb)),
                x_sb=x_sb,
                rhs_gate=lambda j, tg: xln[j][:, 512 * tg:512 * (tg + 1)],
                rhs_up=lambda j, tg: xln[j][:, 512 * tg:512 * (tg + 1)],
                nhc=24, is_int=True, out_scale=1.0 / 127.0, name="f")
    if B_STOP == "ffn":
        dbg_dump_x(); ctx.close(); return nc
    emit_ln(ctx, x_sb, xln, C, SCR)
    if B_STOP == "ln3":
        dbg_dump_xln(); dbg_dump_x(); ctx.close(); return nc
    emit_qkv1(ctx, w1, b1, xln, C, bias_sb, q1t_d, k1t_d, v1_d)
    emit_store_chunks(ctx, x_sb, x3_d)
    ctx.close()
    return nc




L1_GROUPS = [(0, 1, 2), (3, 4, 5), (6, 7, 8), (9, 10, 11), (12, 13, 14), (15,)]


def emit_l1_attention(ctx, q1t_d, k1f, v1f, mask_d, o_out, C):
    nc = ctx.nc
    iters = [(j, km) for j in range(NSTR) for km in range(j + 1)]
    slot_of = {}
    for d_, (j, km) in enumerate(iters):
        slot_of[(j, km)] = d_

    with contextlib.ExitStack() as ast:
        q1t = [ast.enter_context(nc.sbuf_tensor(f"q1_{c}", [128, TPC], BF16))
               for c in range(NCH)]
        kt_b = [ast.enter_context(nc.sbuf_tensor(f"k1b{i}", [128, 4096], BF16)) for i in range(2)]
        v_b = [ast.enter_context(nc.sbuf_tensor(f"v1b{i}", [128, 4096], BF16)) for i in range(2)]
        p2 = [ast.enter_context(nc.sbuf_tensor(f"p2b{i}", [128, 512], BF16)) for i in range(2)]
        mask_sb = ast.enter_context(nc.sbuf_tensor("msk", [128, 512], BF16))
        denr = ast.enter_context(nc.sbuf_tensor("dnr", [1, 128], F32))
        bcb = ast.enter_context(nc.sbuf_tensor("bcr", [128, 128], F32))
        scr8 = ast.enter_context(nc.sbuf_tensor("sc8", [128, 8], BF16))
        o_ps = ast.enter_context(nc.psum_tensor("o1p", [128, 1024], F32))
        s2_ps = [ast.enter_context(nc.psum_tensor(f"s2p{i}", [128, 512], F32))
                 for i in range(2)]
        den_ps = ast.enter_context(nc.psum_tensor("d1p", [1, 128], F32))

        emit_load_chunks(ctx, q1t_d, q1t)

        with nc.Block() as block:
            sdma = ctx.sem("cd")
            smS = ctx.sem("cs")
            sexp = ctx.sem("ce")
            smsk = ctx.sem("ck")
            smPV = ctx.sem("cv")
            sden = ctx.sem("cn")
            sbc = ctx.sem("cb")
            sfin = ctx.sem("cf")
            sd0 = ctx.sem("c0")

            @block.sync
            def _(sy):
                sd0.inc(sy.dma_start(out=mask_sb[:], in_=mask_d[:, :]), 16)
                for d_, (j, km) in enumerate(iters):
                    if d_ >= 2:
                        sy.wait_ge(smPV.h, d_ - 1)
                    sdma.inc(sy.dma_start(
                        out=kt_b[d_ % 2][:],
                        in_=bass.AP(k1f.tensor, 512 * km,
                                    [[SEQ, 128], [128 * SEQ, 8], [1, 512]])), 16)
                    sdma.inc(sy.dma_start(
                        out=v_b[d_ % 2][:],
                        in_=bass.AP(v1f.tensor, D * 512 * km,
                                    [[D, 128], [D * 128, 4], [1, D]])), 16)
                sy.wait_ge(sdma.h, sdma.n)

            ndiag_before = []
            nd = 0
            for (j, km) in iters:
                ndiag_before.append(nd)
                if km == j:
                    nd += 1

            @block.tensor
            def _(t):
                for i, (j, km) in enumerate(iters):
                    d_ = slot_of[(j, km)]
                    t.wait_ge(sdma.h, 32 * (d_ + 1))
                    if j >= 1:
                        t.wait_ge(sfin.h, j)
                    if i >= 2:
                        t.wait_ge(sexp.h, i - 1)
                    for kc in range(4):
                        for dc in range(NCH):
                            ins = t.matmul(s2_ps[i % 2][:, 128 * kc:128 * (kc + 1)],
                                           kt_b[d_ % 2][:, 512 * dc + 128 * kc:
                                                         512 * dc + 128 * (kc + 1)],
                                           q1t[dc][:, 128 * j:128 * (j + 1)],
                                           start=(kc == 0 and dc == 0),
                                           stop=(kc == 3 and dc == NCH - 1))
                    smS.inc(ins, 1)
                    t.wait_ge(sexp.h, i + 1)
                    if km == j:
                        t.wait_ge(smsk.h, ndiag_before[i] + 1)
                    first = (km == 0)
                    last = (km == j)
                    for kc in range(4):
                        psl = p2[i % 2][:, 128 * kc:128 * (kc + 1)]
                        t.matmul(den_ps[0:1, 0:128], C["ones_col"][:, 0:1], psl,
                                 start=(first and kc == 0), stop=(last and kc == 3))
                        for dci in range(NCH):
                            ins = t.matmul(o_ps[:, 128 * dci:128 * (dci + 1)],
                                           v_b[d_ % 2][:, 1024 * kc + 128 * dci:
                                                        1024 * kc + 128 * (dci + 1)],
                                           psl,
                                           start=(first and kc == 0 and dci % 4 == 0),
                                           stop=(last and kc == 3 and dci % 4 == 3))
                    smPV.inc(ins, 1)
                    if km == j:
                        t.wait_ge(sden.h, 2 * (j + 1))
                        sbc.inc(t.matmul(s2_ps[i % 2][:, 0:128], C["ones_row"][0:1, :],
                                         denr[0:1, :], start=True, stop=True), 1)

            @block.scalar
            def _(s):
                for i, (j, km) in enumerate(iters):
                    s.wait_ge(smS.h, i + 1)
                    if i >= 2:
                        s.wait_ge(smPV.h, i - 1)
                    sexp.inc(s.activation(p2[i % 2][:], s2_ps[i % 2][:, 0:512],
                                          AF.Exp, scale=1.0 / 32.0), 1)
                    if km == j:
                        s.wait_ge(sden.h, 2 * j + 1)
                        s.activation(denr[0:1, :], denr[0:1, :], AF.Ln)
                        sden.inc(s.activation(denr[0:1, :], denr[0:1, :], AF.Exp,
                                              scale=-1.0), 1)

            @block.vector
            def _(v):
                v.wait_ge(sd0.h, 16)
                for i, (j, km) in enumerate(iters):
                    if km == j:
                        v.wait_ge(sexp.h, i + 1)
                        v.tensor_tensor(p2[i % 2][:], p2[i % 2][:], mask_sb[:], ALU.mult)
                        smsk.inc(v.tensor_copy(scr8[:], p2[i % 2][:, 0:8]), 1)
                        v.wait_ge(smPV.h, i + 1)
                        sden.inc(v.tensor_copy(denr[0:1, :], den_ps[0:1, 0:128]), 1)
                        v.wait_ge(sbc.h, j + 1)
                        v.tensor_copy(bcb[:], s2_ps[i % 2][:, 0:128])
                        for dci in range(NCH):
                            ins = v.tensor_tensor(o_out[dci][:, 128 * j:128 * (j + 1)],
                                                  o_ps[:, 128 * dci:128 * (dci + 1)],
                                                  bcb[:], ALU.mult)
                        sfin.inc(ins, 1)


def emit_router(ctx, rwT_d, sel_d, xln, C, cwrows, sel_sb):
    nc = ctx.nc
    uid = ctx.uid()
    with contextlib.ExitStack() as rst:
        rwb = rst.enter_context(nc.sbuf_tensor(f"rw{uid}", [128, 64], BF16))
        r_sb = rst.enter_context(nc.sbuf_tensor(f"rs{uid}", [8, 2048], F32))
        sc = rst.enter_context(nc.sbuf_tensor(f"sc{uid}", [128, 8], F32))
        m1 = rst.enter_context(nc.sbuf_tensor(f"m1{uid}", [128, 1], F32))
        m2 = rst.enter_context(nc.sbuf_tensor(f"m2{uid}", [128, 1], F32))
        e1 = rst.enter_context(nc.sbuf_tensor(f"e1{uid}", [128, 8], F32))
        e2 = rst.enter_context(nc.sbuf_tensor(f"e2{uid}", [128, 8], F32))
        t1 = rst.enter_context(nc.sbuf_tensor(f"t1{uid}", [128, 8], F32))
        exs = rst.enter_context(nc.sbuf_tensor(f"ex{uid}", [128, 8], F32))
        cw = rst.enter_context(nc.sbuf_tensor(f"cw{uid}", [128, 8], F32))
        r_ps = rst.enter_context(nc.psum_tensor(f"rp{uid}", [8, 512], F32))
        rt_ps = rst.enter_context(nc.psum_tensor(f"rt{uid}", [128, 8], F32))
        ct_ps = rst.enter_context(nc.psum_tensor(f"ct{uid}", [8, 128], F32))

        with nc.Block() as block:
            sd, sm, sv, ss = ctx.sem("rd"), ctx.sem("rm"), ctx.sem("rv"), ctx.sem("rs")

            @block.gpsimd
            def _(g):
                sd.inc(g.dma_start(out=rwb[:],
                                   in_=bass.AP(rwT_d.tensor, 0,
                                               [[8, 128], [1024, 8], [1, 8]])), 16)
                sd.inc(g.dma_start(out=sel_sb[:], in_=sel_d[:, :]), 16)

            @block.tensor
            def _(t):
                t.wait_ge(sd.h, 32)
                for tg in range(NTG):
                    if tg >= 1:
                        t.wait_ge(sv.h, tg)
                    for dc in range(NCH):
                        ins = t.matmul(r_ps[0:8, 0:512], rwb[:, 8 * dc:8 * (dc + 1)],
                                       xln[dc][:, 512 * tg:512 * (tg + 1)],
                                       start=(dc == 0), stop=(dc == NCH - 1))
                    sm.inc(ins, 1)

            @block.vector
            def _(v):
                for tg in range(NTG):
                    v.wait_ge(sm.h, tg + 1)
                    sv.inc(v.tensor_copy(r_sb[:, 512 * tg:512 * (tg + 1)], r_ps[0:8, 0:512]), 1)

            for ti in range(NSTR):
                @block.tensor
                def _(t, ti=ti):
                    t.wait_ge(sv.h, NTG + 3 * ti)
                    sm.inc(t.transpose(rt_ps[:], r_sb[0:8, 128 * ti:128 * (ti + 1)],
                                       C["eye_f32"][0:8, 0:8]), 1)

                @block.vector
                def _(v, ti=ti):
                    v.wait_ge(sm.h, NTG + 2 * ti + 1)
                    v.tensor_copy(sc[:], rt_ps[:, 0:8])
                    v.reduce_max(m1[:], sc[:], axis=AX.X)
                    v.tensor_scalar(e1[:], sc[:], m1[:, 0:1], None, ALU.is_equal)
                    v.tensor_scalar(t1[:], sc[:], m1[:, 0:1], None, ALU.subtract)
                    sv.inc(v.engine_nop(), 1)

                @block.scalar
                def _(s, ti=ti):
                    s.wait_ge(sv.h, NTG + 3 * ti + 1)
                    ss.inc(s.activation(exs[:], t1[:], AF.Exp), 1)

                @block.vector
                def _(v, ti=ti):
                    v.wait_ge(ss.h, ti + 1)
                    v.scalar_tensor_tensor(t1[:], e1[:], -1e30, sc[:], ALU.mult, ALU.add)
                    v.reduce_max(m2[:], t1[:], axis=AX.X)
                    v.tensor_scalar(e2[:], sc[:], m2[:, 0:1], None, ALU.is_equal)
                    v.tensor_tensor(t1[:], e1[:], e2[:], ALU.add)
                    v.tensor_tensor(t1[:], t1[:], exs[:], ALU.mult)
                    v.reduce_sum(m2[:], t1[:], axis=AX.X)
                    v.reciprocal(m1[:], m2[:])
                    v.tensor_scalar(cw[:], t1[:], m1[:, 0:1], None, ALU.mult)
                    sv.inc(v.engine_nop(), 1)

                @block.tensor
                def _(t, ti=ti):
                    t.wait_ge(sv.h, NTG + 3 * ti + 2)
                    sm.inc(t.transpose(ct_ps[:], cw[:, 0:8], C["eye_f32"][:]), 1)

                @block.vector
                def _(v, ti=ti):
                    v.wait_ge(sm.h, NTG + 2 * ti + 2)
                    sv.inc(v.tensor_copy(cwrows[0:8, 128 * ti:128 * (ti + 1)],
                                         ct_ps[0:8, 0:128]), 1)


C_STOP = None


def build_launch_c():
    nc = bass.Bass("TRN2", target_bir_lowering=False, detect_race_conditions=False)
    ctx = Ctx(nc)
    x3_d = nc.dram_tensor("x3", [D, TPC], F32, kind="ExternalInput").ap()
    q1t_d = nc.dram_tensor("q1t", [D, TPC], BF16, kind="ExternalInput").ap()
    k1f = nc.dram_tensor("k1f", [D, SEQ], BF16, kind="ExternalInput").ap()
    v1f = nc.dram_tensor("v1f", [SEQ, D], BF16, kind="ExternalInput").ap()
    mask_d = nc.dram_tensor("l1mask", [128, 512], BF16, kind="ExternalInput").ap()
    rwT_d = nc.dram_tensor("rwT", [D, 8], F32, kind="ExternalInput").ap()
    sel_d = nc.dram_tensor("sel8", [8, 1024], F32, kind="ExternalInput").ap()
    mwg = nc.dram_tensor("mwg", [8, D, 4096], F32, kind="ExternalInput").ap()
    mwu = nc.dram_tensor("mwu", [8, D, 4096], F32, kind="ExternalInput").ap()
    mwd = nc.dram_tensor("mwd", [8, 4096, D], F32, kind="ExternalInput").ap()
    x5_d = nc.dram_tensor("x5t", [D, TPC], F32, kind="ExternalOutput").ap()

    x_sb = [ctx.sb(f"xc{c}", [128, TPC], F32) for c in range(NCH)]
    xln = [ctx.sb(f"xl{c}", [128, TPC], BF16) for c in range(NCH)]
    C = {"ones_col": ctx.sb("onc", [128, 1], BF16),
         "ones_row": ctx.sb("onr", [1, 128], F32),
         "eye_bf": ctx.sb("eye", [128, 128], BF16),
         "eye_f32": ctx.sb("eyf", [128, 128], F32)}
    SCR = {"xsq": ctx.sb("xsq", [128, TPC], BF16),
           "rows": ctx.sb("rows", [1, 2 * TPC], F32),
           "tmp": ctx.sb("tmpn", [128, 512], F32)}
    cwrows = ctx.sb("cwr", [8, TPC], F32)
    sel_sb = ctx.sb("sel", [8, 1024], F32)

    emit_consts(ctx, C)
    emit_load_chunks(ctx, x3_d, x_sb)
    emit_l1_attention(ctx, q1t_d, k1f, v1f, mask_d, xln, C)

    def dump_xln_and_stop():
        with nc.Block() as block:
            @block.vector
            def _(v):
                for c in range(NCH):
                    v.tensor_copy(x_sb[c][:], xln[c][:])
        emit_store_chunks(ctx, x_sb, x5_d)
        ctx.close()

    if C_STOP == "attn":
        with nc.Block() as block:
            @block.vector
            def _(v):
                for c in range(NCH):
                    v.tensor_copy(x_sb[c][:], xln[c][:])
        emit_store_chunks(ctx, x_sb, x5_d)
        with nc.Block() as block:
            sdd_ = ctx.sem("dbgd")

            @block.sync
            def _(sy):
                sdd_.inc(sy.dma_start(out=x5_d[0:128, 1792:2048], in_=nc._den_dbg[:]), 16)
                sy.wait_ge(sdd_.h, 16)
        ctx.close(); return nc

    with nc.Block() as block:
        @block.vector
        def _(v):
            for c in range(NCH):
                v.tensor_tensor(x_sb[c][:], x_sb[c][:], xln[c][:], ALU.add)

    if C_STOP == "x4":
        emit_store_chunks(ctx, x_sb, x5_d); ctx.close(); return nc

    emit_ln(ctx, x_sb, xln, C, SCR)
    if C_STOP == "ln4":
        dump_xln_and_stop(); return nc
    emit_router(ctx, rwT_d, sel_d, xln, C, cwrows, sel_sb)
    if C_STOP == "router":
        with nc.Block() as block:
            sdr = ctx.sem("drt")

            @block.sync
            def _(sy):
                sdr.inc(sy.dma_start(out=x5_d[0:8, :], in_=cwrows[:]), 16)
                sy.wait_ge(sdr.h, 16)
        ctx.close(); return nc

    with contextlib.ExitStack() as mst:
        cwb_sb = mst.enter_context(nc.sbuf_tensor("cwb", [128, TPC], F32))
        for e in range(8):
            uid = ctx.uid()
            with contextlib.ExitStack() as pst:
                cw_ps = pst.enter_context(nc.psum_tensor(f"cwp{uid}", [128, 512], F32))
                with nc.Block() as block:
                    sm, ss = ctx.sem("xm"), ctx.sem("xs")
                    bm_, bs_ = sm.n, ss.n

                    @block.tensor
                    def _(t):
                        for tg in range(NTG):
                            if tg >= 1:
                                t.wait_ge(ss.h, bs_ + tg)
                            sm.inc(t.matmul(cw_ps[:, 0:512], sel_sb[0:8, 128 * e:128 * (e + 1)],
                                            cwrows[0:8, 512 * tg:512 * (tg + 1)],
                                            start=True, stop=True), 1)

                    @block.scalar
                    def _(s):
                        for tg in range(NTG):
                            s.wait_ge(sm.h, bm_ + tg + 1)
                            ss.inc(s.activation(cwb_sb[:, 512 * tg:512 * (tg + 1)],
                                                cw_ps[:, 0:512], AF.Copy), 1)

            eg = e * D * 4096
            ed = e * 4096 * D
            emit_swiglu(ctx,
                        gate_ap=lambda hc, eg=eg: wcol_ap(mwg, 4096, hc, base=eg),
                        up_ap=lambda hc, eg=eg: wcol_ap(mwu, 4096, hc, base=eg),
                        down_ap=lambda band, hcb, ed=ed: wrow_ap(mwd, 128 * (8 * band + hcb),
                                                                 base=ed),
                        x_sb=x_sb,
                        rhs_gate=lambda j, tg: xln[j][:, 512 * tg:512 * (tg + 1)],
                        rhs_up=lambda j, tg: xln[j][:, 512 * tg:512 * (tg + 1)],
                        nhc=32, is_int=False, out_scale=1.0, name="mo",
                        scale_tile=cwb_sb, tmp_sb=SCR["tmp"])

    emit_store_chunks(ctx, x_sb, x5_d)
    ctx.close()
    return nc




def _circ_matrix(c):
    idx = (np.arange(D)[None, :] - np.arange(D)[:, None]) % D
    return np.ascontiguousarray(np.asarray(c)[idx], dtype=np.float32)


def _core_cols(g):
    tiles = [g + 4 * j for j in range(NSTR)]
    return np.concatenate([np.arange(128 * t, 128 * (t + 1)) for t in tiles])


def kernel(**inputs):
    x = np.ascontiguousarray(np.asarray(inputs["L0_attn_x"]), dtype=np.float32)
    ncore = 8
    cols_of = [_core_cols(c % 4) for c in range(ncore)]

    mq = _circ_matrix(inputs["L0_attn_wq_c"])
    mk = _circ_matrix(inputs["L0_attn_wk_c"])
    mv = _circ_matrix(inputs["L0_attn_wv_c"])
    nc_a = build_launch_a()
    in_a = []
    for c in range(ncore):
        b = c // 4
        in_a.append({
            "xin": np.ascontiguousarray(x[b].T[:, cols_of[c]]),
            "mq": mq, "mk": mk, "mv": mv,
            "bq": np.ascontiguousarray(inputs["L0_attn_wq_b"], dtype=np.float32),
            "bk": np.ascontiguousarray(inputs["L0_attn_wk_b"], dtype=np.float32),
            "bv": np.ascontiguousarray(inputs["L0_attn_wv_b"], dtype=np.float32),
        })
    res_a = run_bass_kernel_spmd(nc_a, in_a, core_ids=list(range(ncore)))
    ra = res_a.results

    k0_full, v0a_full = [], []
    for b in range(2):
        grp = range(4 * b, 4 * b + 4)
        k0 = np.concatenate([np.asarray(ra[c]["k0t"]) for c in grp], axis=1)
        v0 = np.concatenate([np.asarray(ra[c]["v0"]) for c in grp], axis=0)
        v0a = np.ones((SEQ, 1040), dtype=BFD)
        for h in range(16):
            v0a[:, 65 * h:65 * h + 64] = v0[:, 64 * h:64 * h + 64]
        k0_full.append(np.ascontiguousarray(k0))
        v0a_full.append(np.ascontiguousarray(v0a))

    wgT = np.ascontiguousarray(np.asarray(inputs["L0_ffn_w_gate"]).T.astype(np.int32))
    wuT = np.ascontiguousarray(np.asarray(inputs["L0_ffn_w_up"]).T.astype(np.int32))
    wdT = np.ascontiguousarray(np.asarray(inputs["L0_ffn_w_down"]).T.astype(np.int32))
    w1 = np.ascontiguousarray(inputs["L1_attn_W"], dtype=np.float32)
    b1 = np.ascontiguousarray(inputs["L1_attn_b"], dtype=np.float32)
    nc_b = build_launch_b()
    in_b = []
    for c in range(ncore):
        b = c // 4
        in_b.append({
            "xin": in_a[c]["xin"],
            "q0t": np.asarray(ra[c]["q0t"]),
            "k0f": k0_full[b], "v0a": v0a_full[b],
            "mo": _circ_matrix(inputs["L0_attn_wo_c"]),
            "bo": np.ascontiguousarray(inputs["L0_attn_wo_b"], dtype=np.float32),
            "wgT": wgT, "wuT": wuT, "wdT": wdT, "w1": w1, "b1": b1,
        })
    res_b = run_bass_kernel_spmd(nc_b, in_b, core_ids=list(range(ncore)))
    rb = res_b.results

    k1_full = [np.empty((D, SEQ), dtype=BFD) for _ in range(2)]
    v1_full = [np.empty((SEQ, D), dtype=BFD) for _ in range(2)]
    for b in range(2):
        for T in range(64):
            c = 4 * b + (T % 4)
            j = T // 4
            k1_full[b][:, 128 * T:128 * (T + 1)] = np.asarray(rb[c]["k1t"])[:, 128 * j:128 * (j + 1)]
            v1_full[b][128 * T:128 * (T + 1), :] = np.asarray(rb[c]["v1"])[128 * j:128 * (j + 1), :]

    rwT = np.ascontiguousarray(np.asarray(inputs["L1_ffn_router_w"]).T.astype(np.float32))
    sel8 = np.zeros((8, 1024), dtype=np.float32)
    for e in range(8):
        sel8[e, 128 * e:128 * (e + 1)] = 1.0
    mwg = np.ascontiguousarray(inputs["L1_ffn_w_gate"], dtype=np.float32)
    mwu = np.ascontiguousarray(inputs["L1_ffn_w_up"], dtype=np.float32)
    mwd = np.ascontiguousarray(inputs["L1_ffn_w_down"], dtype=np.float32)
    krel = np.arange(128)[:, None]
    qrel = np.arange(128)[None, :]
    nc_c = build_launch_c()
    in_c = []
    for c in range(ncore):
        b, g = c // 4, c % 4
        mask = np.concatenate(
            [(krel - qrel <= 128 * (g - kc)) for kc in range(4)],
            axis=1).astype(ml_dtypes.bfloat16)
        in_c.append({
            "x3": np.asarray(rb[c]["x3"]),
            "q1t": np.asarray(rb[c]["q1t"]),
            "k1f": np.ascontiguousarray(k1_full[b]),
            "v1f": np.ascontiguousarray(v1_full[b]),
            "l1mask": mask, "rwT": rwT, "sel8": sel8,
            "mwg": mwg, "mwu": mwu, "mwd": mwd,
        })
    res_c = run_bass_kernel_spmd(nc_c, in_c, core_ids=list(range(ncore)))

    out = np.zeros((2, SEQ, D), dtype=np.float32)
    for c in range(ncore):
        b = c // 4
        out[b][cols_of[c], :] = np.asarray(res_c.results[c]["x5t"]).T
    return out
